# version 01
# baseline (speedup 1.0000x reference)
import numpy as np
import ml_dtypes
import concourse.bass as bass
import concourse.mybir as mybir
from concourse.bass_utils import run_bass_kernel_spmd

F32 = mybir.dt.float32
BF16 = mybir.dt.bfloat16
AF = mybir.ActivationFunctionType
ALU = mybir.AluOpType
AX = mybir.AxisListType.X

D = 2048
NCH = 16
TCORE = 1024
NTILE = 8
SEQ = 4096
BF = ml_dtypes.bfloat16
RMS_EPS = float(np.finfo(np.float32).eps)

DEBUG_DUMPS = False
ENGINES = ("sync", "gpsimd", "scalar", "vector", "tensor")
SV, SS, SG = "self_vector", "self_scalar", "self_gpsimd"


class Emit:
    def __init__(self, nc):
        self.nc = nc
        self.sems = {}
        self.counts = {}
        self.pass_engine = None
        self.eng = None
        self.last_waited = {}
        for s in (SV, SS, SG):
            self._sem(s)

    def _sem(self, name):
        if name not in self.sems:
            self.sems[name] = self.nc.alloc_semaphore(name)
            self.counts[name] = 0
        return name

    def c(self, name):
        self._sem(name)
        return self.counts[name]

    def run(self, blk, fn):
        for en in ENGINES:
            def section(eng, en=en):
                self.pass_engine = en
                self.eng = eng
                for k in self.counts:
                    self.counts[k] = 0
                self.last_waited = {}
                self.marks = {}
                fn(self)
            getattr(blk, en)(section)

    def mark(self, name, sem):
        self.marks.setdefault(name, []).append(self.counts[sem])

    def markv(self, name, i):
        lst = self.marks.get(name, [])
        return lst[i] if 0 <= i < len(lst) else 0

    def _emit_waits(self, wl):
        for s, v in wl:
            self._sem(s)
            if v > 0 and self.last_waited.get(s, -1) < v:
                self.eng.wait_ge(self.sems[s], v)
                self.last_waited[s] = v

    def op(self, engine, fn, waits=(), inc=None, chain=True):
        selfsem = {"scalar": SS, "vector": SV, "gpsimd": SG}.get(engine)
        wl = [(s, v) for (s, v) in waits]
        if selfsem is not None:
            assert inc is None
            if chain:
                wl.append((selfsem, self.counts[selfsem]))
            inc = (selfsem, 1)
        if self.pass_engine == engine:
            self._emit_waits(wl)
            inst = fn(self.eng)
            if inc is not None:
                self._sem(inc[0])
                inst.then_inc(self.sems[inc[0]], inc[1])
                self.counts[inc[0]] += inc[1]
        else:
            if inc is not None:
                self._sem(inc[0])
                self.counts[inc[0]] += inc[1]

    def wait(self, engine, waits):
        for s, v in waits:
            self._sem(s)
        if self.pass_engine == engine:
            self._emit_waits([(s, v) for (s, v) in waits])

    def dma(self, queue, out_ap, in_ap, sem, waits=(), chain=False, **kw):
        wl = list(waits)
        if chain and queue == "gpsimd":
            wl.append((SG, self.c(SG)))
        if self.pass_engine == queue:
            self._emit_waits(wl)
            inst = self.eng.dma_start(out_ap, in_ap, **kw)
            self._sem(sem)
            inst.then_inc(self.sems[sem], 16)
            self.counts[sem] += 16
        else:
            for s, v in wl:
                self._sem(s)
            self._sem(sem)
            self.counts[sem] += 16



def core_tiles(core):
    return [4 * k + (core % 4) for k in range(8)]


def core_token_index(core):
    return np.concatenate([np.arange(t * 128, (t + 1) * 128) for t in core_tiles(core)])


def shard_x(x_full):
    x_full = np.asarray(x_full)
    return [np.ascontiguousarray(x_full[0 if c < 4 else 1, core_token_index(c), :])
            for c in range(8)]


def unshard_x(per_core):
    full = np.zeros((2, SEQ, D), dtype=per_core[0].dtype)
    for c in range(8):
        full[0 if c < 4 else 1, core_token_index(c), :] = per_core[c]
    return full


def pack_lhsT_tiles(w):
    dd, ff = w.shape
    nk, nj = dd // 128, ff // 128
    return np.ascontiguousarray(
        w.reshape(nk, 128, nj, 128).transpose(2, 1, 0, 3).reshape(nj, 128, nk * 128))


def build_l0_mask01():
    kl = np.arange(128)
    ql = np.arange(128)
    same = (kl[:, None] // 32) == (ql[None, :] // 32)
    tok = (same & ((kl[:, None] % 32) <= (ql[None, :] % 32) - 16)).astype(np.float32)
    pm = (np.arange(16)[:, None] <= (ql[None, :] % 32)).astype(np.float32)
    return (np.ascontiguousarray(np.tile(tok, (1, 4)).astype(BF)),
            np.ascontiguousarray(np.tile(pm, (1, 4)).astype(BF)))


def prep_l0_weights(inp):
    w_qkv = np.asarray(inp["L0_attn_w_qkv"])
    wq, wk, wv = w_qkv[:, 0:D], w_qkv[:, D:2 * D], w_qkv[:, 2 * D:3 * D]
    pm = np.asarray(inp["L0_attn_pm"])
    pm_k = np.ascontiguousarray(
        pm[0].reshape(16, 2, 16, 64).transpose(1, 3, 0, 2).reshape(128, 256))
    pm_v = np.ascontiguousarray(pm[1].transpose(1, 0, 2).reshape(16, 32 * 64))
    a_arr = np.asarray(inp["L0_ffn_A_w"]).reshape(64, D).T
    r_arr = np.asarray(inp["L0_ffn_W_router"]).T
    mask_tok, mask_pm = build_l0_mask01()
    return {
        "wq_p": pack_lhsT_tiles(wq),
        "wk_p": pack_lhsT_tiles(wk),
        "wv": np.ascontiguousarray(wv),
        "w_out": np.ascontiguousarray(np.asarray(inp["L0_attn_w_out"])),
        "w_base_t": np.ascontiguousarray(np.asarray(inp["L0_ffn_W_base"]).T),
        "b_base": np.ascontiguousarray(np.asarray(inp["L0_ffn_b_base"]).reshape(1, D)),
        "w_norm": np.ascontiguousarray(np.asarray(inp["L0_attn_w_norm"]).reshape(1, D)),
        "ar_arr": np.ascontiguousarray(np.concatenate([a_arr, r_arr], axis=1)),
        "b_arr": np.ascontiguousarray(
            np.asarray(inp["L0_ffn_B_w"]).transpose(0, 2, 1).reshape(64, D)),
        "pm_k": pm_k,
        "pm_v": pm_v,
        "mask_tok": mask_tok,
        "mask_pm": mask_pm,
        "ident": np.ascontiguousarray(np.eye(128, dtype=BF)),
    }


L0_W_SPECS = {
    "wq_p": ([16, 128, 2048], F32), "wk_p": ([16, 128, 2048], F32),
    "wv": ([D, D], F32), "w_out": ([D, D], F32), "w_base_t": ([D, D], F32),
    "b_base": ([1, D], F32), "w_norm": ([1, D], F32), "ar_arr": ([D, 72], F32),
    "b_arr": ([64, D], F32), "pm_k": ([128, 256], F32), "pm_v": ([16, 2048], F32),
    "mask_tok": ([128, 512], BF16), "mask_pm": ([16, 512], BF16),
    "ident": ([128, 128], BF16),
}

R_QM4, R_KM4, R_PMN, R_C4, R_TMP = 0, 4, 8, 12, 16


def declare_l0_tensors(nc):
    d_ = {
        "x": nc.dram_tensor("x_in", [TCORE, D], F32, kind="ExternalInput"),
        "x1": nc.dram_tensor("x1", [TCORE, D], F32, kind="ExternalOutput"),
    }
    if DEBUG_DUMPS:
        d_["dbg1"] = nc.dram_tensor("dbg1", [128, NTILE * D], BF16, kind="ExternalOutput")
        d_["dbg2"] = nc.dram_tensor("dbg2", [128, 64], F32, kind="ExternalOutput")
        d_["dbg3"] = nc.dram_tensor("dbg3", [128, 9216], BF16, kind="ExternalOutput")
        d_["dbg4"] = nc.dram_tensor("dbg4", [128, 2048], BF16, kind="ExternalOutput")
    for name, (shape, dt) in L0_W_SPECS.items():
        d_[name] = nc.dram_tensor("w_" + name, shape, dt, kind="ExternalInput")

    A = nc.alloc_sbuf_tensor
    t_ = {"dram": d_}
    t_["x"] = A("x_sb", [128, NTILE * D], F32)
    t_["bfA"] = A("bfA", [128, NCH * 1024], BF16)
    t_["bfB"] = A("bfB", [128, NTILE * D], BF16)
    t_["scr"] = A("scr", [128, 4608], F32)
    t_["scrb"] = t_["scr"].bitcast(BF16)
    t_["wbuf"] = A("wbuf", [128, 5120], BF16)
    t_["ctxf"] = A("ctxf", [128, 4 * D], BF16)
    t_["ctxt"] = A("ctxt", [128, D], BF16)
    t_["etok"] = A("etok", [128, 1024], BF16)
    t_["epm"] = A("epm", [128, 512], BF16)
    t_["mtok"] = A("mtok", [128, 512], BF16)
    t_["mpm"] = A("mpm", [16, 512], BF16)
    t_["pmk"] = A("pmk", [128, 256], BF16)
    t_["pmv"] = A("pmv", [128, 2048], BF16)
    t_["lora"] = A("lora", [128, 16 * 72], BF16)
    t_["barr"] = A("barr", [64, 2048], BF16)
    t_["bbase"] = A("bbase", [1, 2048], BF16)
    t_["wnormb"] = A("wnormb", [128, 2048], BF16)
    t_["gall"] = A("gall", [128, 64], F32)
    t_["hbf"] = A("hbf", [128, 512], BF16)
    t_["hpf"] = A("hpf", [64, 1024], BF16)
    t_["ident"] = A("ident_sb", [128, 128], BF16)
    t_["ones_col"] = A("ones_col", [128, 1], BF16)
    t_["ones_row_b"] = A("ones_row_b", [1, 128], BF16)
    t_["ones_row_f"] = A("ones_row_f", [1, 128], F32)
    t_["stats"] = A("stats", [128, 64], F32)
    t_["rows"] = A("rows", [1, 64], F32)
    t_["negc"] = A("negc", [128, 8], F32)
    t_["consts"] = A("consts", [128, 2], F32)

    t_["psA"] = nc.alloc_psum_tensor("psA", [128, 2048], F32)
    t_["psB"] = nc.alloc_psum_tensor("psB", [128, 1024], F32)
    t_["psC"] = nc.alloc_psum_tensor("psC", [128, 512], F32)
    t_["psD"] = nc.alloc_psum_tensor("psD", [128, 512], F32)
    t_["psCb"] = t_["psC"].bitcast(BF16)
    t_["psDb"] = t_["psD"].bitcast(BF16)
    return t_


def _ypan(t_, tl):
    if tl < 4:
        return t_["psA"][:, tl * 512:(tl + 1) * 512]
    if tl < 6:
        return t_["psB"][:, (tl - 4) * 512:(tl - 3) * 512]
    return (t_["psC"] if tl == 6 else t_["psD"])[:, 0:512]


def transpose_128(e, t_, src_ap_fn, dst_ap_fn, n, tr_sem, extra_tensor_waits=()):
    for i in range(n):
        pst = t_["psCb"] if i % 2 == 0 else t_["psDb"]
        src = src_ap_fn(i)
        w = src.shape[-1]
        waits = [(SV, max(0, e.c(SV) - 1))] + (list(extra_tensor_waits) if i == 0 else [])
        e.op("tensor", lambda t, src=src, pst=pst, w=w: t.transpose(
            pst[0:w, 0:128], src, t_["ident"][:]), waits=waits, inc=(tr_sem, 1))
        dst = dst_ap_fn(i)
        e.op("vector", lambda v, dst=dst, pst=pst, w=w: v.tensor_copy(
            dst, pst[0:w, 0:128]),
            waits=[(tr_sem, e.c(tr_sem))])


def emit_l0_block(e, t_, with_attn=True):
    d_ = t_["dram"]
    x, xnf, xnt = t_["x"], t_["bfA"], t_["bfB"]
    scr, scrb, wbuf = t_["scr"], t_["scrb"], t_["wbuf"]
    psA, psB, psC, psD = t_["psA"], t_["psB"], t_["psC"], t_["psD"]
    stats, rows = t_["stats"], t_["rows"]

    e.op("gpsimd", lambda g: g.memset(t_["ones_col"][:], 1.0))
    e.op("gpsimd", lambda g: g.memset(t_["consts"][:, 0:1], 1e-5))
    e.op("gpsimd", lambda g: g.memset(t_["consts"][:, 1:2], RMS_EPS))
    e.op("gpsimd", lambda g: g.memset(t_["ones_row_b"][:], 1.0))
    e.op("gpsimd", lambda g: g.memset(t_["ones_row_f"][:], 1.0))
    e.op("gpsimd", lambda g: g.memset(t_["epm"][:], 0.0))
    e.op("gpsimd", lambda g: g.memset(t_["pmv"][:], 0.0))
    e.dma("sync", x[:].rearrange("p (t d) -> p t d", d=D),
          d_["x"].rearrange("(t p) d -> p t d", p=128), "s_xld")
    e.dma("sync", t_["mtok"][:], d_["mask_tok"][:], "s_cld")
    e.dma("sync", t_["mpm"][:], d_["mask_pm"][:], "s_cld")
    e.dma("sync", t_["ident"][:], d_["ident"][:], "s_cld")
    e.dma("gpsimd", t_["pmk"][:], d_["pm_k"][:], "s_cld", chain=True)
    e.dma("gpsimd", t_["pmv"][0:16, :], d_["pm_v"][:], "s_cld", chain=True)
    e.dma("gpsimd", t_["lora"][:].rearrange("p (k f) -> p k f", f=72),
          d_["ar_arr"].rearrange("(k p) f -> p k f", p=128), "s_cld", chain=True)
    e.dma("gpsimd", t_["barr"][:], d_["b_arr"][:], "s_cld", chain=True)
    e.dma("gpsimd", t_["bbase"][:], d_["b_base"][:], "s_cld", chain=True)
    e.dma("gpsimd", t_["wnormb"][0:1, :], d_["w_norm"][:], "s_cld", chain=True)
    NCLD = e.c("s_cld")
    e.wait("tensor", [("s_cld", NCLD)])
    for g4 in range(4):
        e.op("tensor", lambda t, g4=g4: t.matmul(
            psA[:, g4 * 512:(g4 + 1) * 512], t_["ones_row_b"][:],
            t_["wnormb"][0:1, g4 * 512:(g4 + 1) * 512], start=True, stop=True),
            inc=(("m_wn", 1) if g4 == 3 else None))
    e.op("vector", lambda v: v.tensor_copy(t_["wnormb"][:], psA[:, 0:2048]),
         waits=[("m_wn", 1)])
    V_WN = e.c(SV)

    e.wait("scalar", [("s_xld", 16)])
    e.wait("vector", [("s_xld", 16)])
    for tl in range(NTILE):
        xt = x[:, tl * D:(tl + 1) * D]
        e.op("scalar", lambda s, xt=xt: s.activation(
            scr[:, 0:2048], xt, AF.Square, accum_out=stats[:, 0:1]))
        e.op("vector", lambda v, xt=xt: v.reduce_sum(stats[:, 1:2], xt, axis=AX))
        e.op("vector", lambda v: v.tensor_scalar_mul(stats[:, 2:3], stats[:, 1:2], 1.0 / D))
        e.op("vector", lambda v: v.tensor_mul(stats[:, 3:4], stats[:, 2:3], stats[:, 2:3]))
        e.op("vector", lambda v: v.scalar_tensor_tensor(
            stats[:, 3:4], stats[:, 0:1], 1.0 / D, stats[:, 3:4], ALU.mult, ALU.subtract),
            waits=[(SS, e.c(SS))])
        e.op("scalar", lambda s: s.activation(stats[:, 3:4], stats[:, 3:4], AF.Ln, bias=t_["consts"][:, 0:1]),
             waits=[(SV, e.c(SV))])
        e.op("scalar", lambda s: s.activation(stats[:, 4:5], stats[:, 3:4], AF.Exp, scale=-0.5))
        e.op("vector", lambda v: v.scalar_tensor_tensor(
            stats[:, 5:6], stats[:, 2:3], -1.0, stats[:, 4:5], ALU.mult, ALU.mult),
            waits=[(SS, e.c(SS))])
        xnt_t = xnt[:, tl * D:(tl + 1) * D]
        e.op("scalar", lambda s, xt=xt, xnt_t=xnt_t: s.activation(
            xnt_t, xt, AF.Identity, bias=stats[:, 5:6], scale=stats[:, 4:5]),
            waits=[(SV, e.c(SV))])
        S_LN = e.c(SS)
        transpose_128(
            e, t_,
            lambda ch, xnt_t=xnt_t: xnt_t[:, ch * 128:(ch + 1) * 128],
            lambda ch, tl=tl: xnf[:, ch * 1024 + tl * 128: ch * 1024 + (tl + 1) * 128],
            NCH, "m_tr",
            extra_tensor_waits=[(SS, S_LN), ("s_cld", NCLD), (SV, V_WN)])
    V_XNF = e.c(SV)

    e.wait("tensor", [(SV, V_XNF)])
    for tl in range(NTILE):
        for k in range(NCH):
            e.op("tensor", lambda t, tl=tl, k=k: t.matmul(
                psB[:, tl * 128:tl * 128 + 72],
                xnf[:, k * 1024 + tl * 128: k * 1024 + (tl + 1) * 128],
                t_["lora"][:, k * 72:(k + 1) * 72],
                start=(k == 0), stop=(k == NCH - 1)),
                inc=(("m_hl", 1) if (tl == NTILE - 1 and k == NCH - 1) else None))
    lg3 = scr[:, 0:64].rearrange("p (t e) -> p t e", e=8)
    is1 = scr[:, 64:128].rearrange("p (t e) -> p t e", e=8)
    l2 = scr[:, 128:192].rearrange("p (t e) -> p t e", e=8)
    is2 = scr[:, 192:256].rearrange("p (t e) -> p t e", e=8)
    g3 = t_["gall"][:].rearrange("p (t e) -> p t e", e=8)
    m1, m2 = stats[:, 8:16], stats[:, 16:24]
    d12, p1, p2 = stats[:, 24:32], stats[:, 32:40], stats[:, 40:48]
    m1b = m1.rearrange("p (t o) -> p t o", o=1).broadcast_to([128, 8, 8])
    m2b = m2.rearrange("p (t o) -> p t o", o=1).broadcast_to([128, 8, 8])
    p1b = p1.rearrange("p (t o) -> p t o", o=1).broadcast_to([128, 8, 8])
    p2b = p2.rearrange("p (t o) -> p t o", o=1).broadcast_to([128, 8, 8])
    hl3 = psB[:, 0:8 * 128].rearrange("p (t f) -> p t f", f=128)
    e.op("vector", lambda v: v.tensor_copy(lg3, hl3[:, :, 64:72]), waits=[("m_hl", 1)])
    e.op("vector", lambda v: v.reduce_max(m1, lg3, axis=AX))
    e.op("vector", lambda v: v.tensor_tensor(is1, lg3, m1b, ALU.is_ge))
    e.op("vector", lambda v: v.scalar_tensor_tensor(l2, is1, -1e30, lg3, ALU.mult, ALU.add))
    e.op("vector", lambda v: v.reduce_max(m2, l2, axis=AX))
    e.op("vector", lambda v: v.tensor_sub(d12, m1, m2))
    e.op("scalar", lambda s: s.activation(p1, d12, AF.Sigmoid), waits=[(SV, e.c(SV))])
    e.op("vector", lambda v: v.tensor_scalar(p2, p1, -1.0, 1.0, ALU.mult, ALU.add),
         waits=[(SS, e.c(SS))])
    e.op("vector", lambda v: v.tensor_tensor(is2, l2, m2b, ALU.is_ge))
    e.op("vector", lambda v: v.tensor_tensor(is1, is1, p1b, ALU.mult))
    e.op("vector", lambda v: v.tensor_tensor(is2, is2, p2b, ALU.mult))
    e.op("vector", lambda v: v.tensor_add(g3, is1, is2))
    h4 = hl3[:, :, 0:64].rearrange("p t (ee r) -> p t ee r", r=8)
    hb4 = t_["hbf"][:].rearrange("p (t ee r) -> p t ee r", ee=8, r=8)
    g4b = t_["gall"][:].rearrange("p (t ee o) -> p t ee o", t=8, o=1).broadcast_to([128, 8, 8, 8])
    e.op("vector", lambda v: v.scalar_tensor_tensor(hb4, h4, 2.0, g4b, ALU.mult, ALU.mult))
    V_HB = e.c(SV)
    transpose_128(
        e, t_,
        lambda tl: t_["hbf"][:, tl * 64:(tl + 1) * 64],
        lambda tl: t_["hpf"][0:64, tl * 128:(tl + 1) * 128],
        NTILE, "m_hp", extra_tensor_waits=[(SV, V_HB)])
    V_HPF = e.c(SV)

    for pn in range(4):
        for k in range(NCH):
            seq = pn * NCH + k
            slot = wbuf[:, (seq % 4) * 512:((seq % 4) + 1) * 512]
            e.dma("gpsimd", slot,
                  d_["w_base_t"][k * 128:(k + 1) * 128, pn * 512:(pn + 1) * 512],
                  "s_wb", waits=[("m_yk", max(0, seq - 3))], chain=True)
            wv_ = ([(SV, e.c(SV))] if k == 0 else [])
            e.wait("tensor", [("s_wb", (seq + 1) * 16)] + wv_)
            for tl in range(NTILE):
                e.op("tensor", lambda t, tl=tl, k=k, slot=slot: t.matmul(
                    _ypan(t_, tl),
                    xnf[:, k * 1024 + tl * 128: k * 1024 + (tl + 1) * 128],
                    slot, start=(k == 0), stop=False),
                    inc=(("m_yk", 1) if tl == NTILE - 1 else None))
        for tl in range(NTILE):
            yp = _ypan(t_, tl)
            e.op("tensor", lambda t, yp=yp, pn=pn: t.matmul(
                yp, t_["ones_row_b"][:], t_["bbase"][0:1, pn * 512:(pn + 1) * 512],
                start=False, stop=False))
            e.op("tensor", lambda t, tl=tl, yp=yp, pn=pn: t.matmul(
                yp, t_["hpf"][0:64, tl * 128:(tl + 1) * 128],
                t_["barr"][0:64, pn * 512:(pn + 1) * 512], start=False, stop=True),
                inc=("m_y", 1))
        base_my = e.c("m_y") - NTILE
        for tl in range(NTILE):
            yp = _ypan(t_, tl)
            e.op("vector", lambda v, tl=tl, yp=yp, pn=pn: v.tensor_add(
                x[:, tl * D + pn * 512: tl * D + (pn + 1) * 512],
                x[:, tl * D + pn * 512: tl * D + (pn + 1) * 512], yp),
                waits=[("m_y", base_my + tl + 1)])
    V_FFN = e.c(SV)

    if with_attn:
        emit_l0_attn(e, t_, V_FFN)

    if DEBUG_DUMPS:
        e.dma("sync", d_["dbg1"][:], xnt[:], "s_dbg", waits=[(SV, e.c(SV)), (SS, e.c(SS))])
        e.dma("sync", d_["dbg3"][:], t_["scrb"][:], "s_dbg")
        e.dma("sync", d_["dbg4"][:], t_["ctxt"][:], "s_dbg")
        e.dma("sync", d_["dbg2"][:, 0:8], t_["negc"][:], "s_dbg")
    e.dma("sync", d_["x1"].rearrange("(t p) d -> p t d", p=128),
          x[:].rearrange("p (t d) -> p t d", d=D),
          "s_xout", waits=[(SV, e.c(SV))])
    e.wait("sync", [("s_xout", e.c("s_xout"))])


def emit_l0_attn(e, t_, V_FFN):
    d_ = t_["dram"]
    x, xnf, xnt = t_["x"], t_["bfA"], t_["bfB"]
    scr, scrb, wbuf = t_["scr"], t_["scrb"], t_["wbuf"]
    psA, psB, psC, psD = t_["psA"], t_["psB"], t_["psC"], t_["psD"]
    stats, rows = t_["stats"], t_["rows"]
    QOF, KOF = 0, NTILE * D // 2
    V_OF = 0

    for tl in range(NTILE):
        xnt_t = xnt[:, tl * D:(tl + 1) * D]
        e.op("scalar", lambda s, xnt_t=xnt_t: s.activation(
            scrb[:, 0:2048], xnt_t, AF.Square, accum_out=stats[:, 0:1]),
            waits=[(SV, V_FFN)] if tl == 0 else [])
        e.op("scalar", lambda s: s.activation(
            stats[:, 1:2], stats[:, 0:1], AF.Ln, bias=t_["consts"][:, 1:2], scale=1.0 / D))
        e.op("scalar", lambda s: s.activation(stats[:, 2:3], stats[:, 1:2], AF.Exp, scale=-0.5))
        e.op("scalar", lambda s, xnt_t=xnt_t: s.activation(
            scrb[:, 2048:4096], xnt_t, AF.Identity, scale=stats[:, 2:3]),
            waits=[(SV, e.c(SV))])
        e.op("vector", lambda v, xnt_t=xnt_t: v.tensor_mul(
            xnt_t, scrb[:, 2048:4096], t_["wnormb"][:]), waits=[(SS, e.c(SS))])
        V_XA = e.c(SV)
        transpose_128(
            e, t_,
            lambda ch, xnt_t=xnt_t: xnt_t[:, ch * 128:(ch + 1) * 128],
            lambda ch, tl=tl: xnf[:, ch * 1024 + tl * 128: ch * 1024 + (tl + 1) * 128],
            NCH, "m_tr2", extra_tensor_waits=[(SV, V_XA)])

    e.op("scalar", lambda s: s.activation(scrb[:, 4096:4352], t_["pmk"][:], AF.Square))
    S_PMN = e.c(SS)
    e.wait("tensor", [(SS, S_PMN), (SV, e.c(SV))])
    for ch in range(NCH):
        e.op("tensor", lambda t, ch=ch: t.matmul(
            psD[0:1, ch * 16:(ch + 1) * 16], t_["ones_col"][:],
            scrb[:, 4096 + ch * 16: 4096 + (ch + 1) * 16], start=True, stop=True),
            inc=(("m_pmn", 1) if ch == NCH - 1 else None))
    e.op("vector", lambda v: v.reduce_max(
        rows[0:1, R_PMN:R_PMN + 1], psD[0:1, 0:256], axis=AX), waits=[("m_pmn", 1)])
    V_PMN = e.c(SV)

    for half in range(2):
        htiles = [half * 4 + i for i in range(4)]
        e.wait("tensor", [(SV, e.c(SV))])
        for wname, dof, scale in [("wq_p", QOF, 0.125), ("wk_p", KOF, 1.0)]:
            for j in range(NCH):
                seq = e.c("s_qkw") // 16
                slot = wbuf[:, (seq % 2) * 2048:((seq % 2) + 1) * 2048]
                e.dma("gpsimd", slot, d_[wname][j], "s_qkw",
                      waits=[("m_qk", max(0, seq - 1)), ("m_yk", e.c("m_yk")),
                             ("m_ow", e.c("m_ow"))], chain=True)
                pss = psB[:, (seq % 2) * 512:((seq % 2) + 1) * 512]
                e.wait("tensor", [("s_qkw", (seq + 1) * 16),
                                  (SS, e.markv("qke", seq - 2))])
                for k in range(NCH):
                    e.op("tensor", lambda t, k=k, slot=slot, pss=pss, half=half: t.matmul(
                        pss, slot[:, k * 128:(k + 1) * 128],
                        xnf[:, k * 1024 + half * 512: k * 1024 + half * 512 + 512],
                        start=(k == 0), stop=(k == NCH - 1)),
                        inc=(("m_qk", 1) if k == NCH - 1 else None))
                dst = xnt[:, dof + j * 512: dof + (j + 1) * 512]
                e.op("scalar", lambda s, dst=dst, pss=pss, scale=scale: s.activation(
                    dst, pss, AF.Copy, scale=scale), waits=[("m_qk", e.c("m_qk"))])
                e.mark("qke", SS)
        S_QK = e.c(SS)
        for g4 in range(4):
            for k in range(NCH):
                seq = e.c("s_vw") // 16
                slot = wbuf[:, 4096 + (seq % 2) * 512:4096 + ((seq % 2) + 1) * 512]
                e.dma("gpsimd", slot,
                      d_["wv"][k * 128:(k + 1) * 128, g4 * 512:(g4 + 1) * 512],
                      "s_vw", waits=[("m_vk", max(0, e.c("m_vk") - 1))], chain=True)
                wv_ = ([(SV, e.c(SV))] if k == 0 else [])
                e.wait("tensor", [("s_vw", (seq + 1) * 16)] + wv_)
                for i in range(4):
                    e.op("tensor", lambda t, i=i, k=k, slot=slot, half=half: t.matmul(
                        psA[:, i * 512:(i + 1) * 512],
                        xnf[:, k * 1024 + half * 512 + i * 128:
                            k * 1024 + half * 512 + (i + 1) * 128],
                        slot, start=(k == 0), stop=(k == NCH - 1)),
                        inc=(("m_vk", 1) if i == 3 else None))
            base_mv = e.c("m_vk")
            for i in range(4):
                e.op("vector", lambda v, i=i, g4=g4: v.tensor_copy(
                    scrb[:, V_OF + i * 2048 + g4 * 512: V_OF + i * 2048 + (g4 + 1) * 512],
                    psA[:, i * 512:(i + 1) * 512]),
                    waits=[("m_vk", base_mv)])
        V_VE = e.c(SV)

        for src_of, rof in [(QOF, R_QM4), (KOF, R_KM4)]:
            for ch in range(NCH):
                e.op("scalar", lambda s, src_of=src_of, ch=ch: s.activation(
                    scrb[:, 8192:8704],
                    xnt[:, src_of + ch * 512: src_of + (ch + 1) * 512], AF.Square),
                    waits=[("m_n2", e.c("m_n2"))])
                e.op("tensor", lambda t: t.matmul(
                    psC[0:1, 0:512], t_["ones_col"][:], scrb[:, 8192:8704],
                    start=True, stop=True),
                    waits=[(SS, e.c(SS)), (SV, e.c(SV))], inc=("m_n2", 1))
                e.op("vector", lambda v: v.reduce_max(
                    rows[0:1, R_TMP:R_TMP + 4],
                    psC[0:1, 0:512].rearrange("o (a b) -> o a b", a=4), axis=AX),
                    waits=[("m_n2", e.c("m_n2"))])
                if ch == 0:
                    e.op("vector", lambda v, rof=rof: v.tensor_copy(
                        rows[0:1, rof:rof + 4], rows[0:1, R_TMP:R_TMP + 4]))
                else:
                    e.op("vector", lambda v, rof=rof: v.tensor_max(
                        rows[0:1, rof:rof + 4], rows[0:1, rof:rof + 4],
                        rows[0:1, R_TMP:R_TMP + 4]))
        e.op("vector", lambda v: v.tensor_max(
            rows[0:1, R_KM4:R_KM4 + 4], rows[0:1, R_KM4:R_KM4 + 4],
            rows[0:1, R_PMN:R_PMN + 1].broadcast_to([1, 4])))
        e.op("vector", lambda v: v.tensor_mul(
            rows[0:1, R_C4:R_C4 + 4], rows[0:1, R_QM4:R_QM4 + 4],
            rows[0:1, R_KM4:R_KM4 + 4]))
        e.op("scalar", lambda s: s.activation(
            rows[0:1, R_C4:R_C4 + 4], rows[0:1, R_C4:R_C4 + 4], AF.Ln),
            waits=[(SV, e.c(SV))])
        e.op("scalar", lambda s: s.activation(
            rows[0:1, R_C4:R_C4 + 4], rows[0:1, R_C4:R_C4 + 4], AF.Exp, scale=0.5))
        e.op("vector", lambda v: v.tensor_scalar_mul(
            rows[0:1, R_C4:R_C4 + 4], rows[0:1, R_C4:R_C4 + 4], -1.0),
            waits=[(SS, e.c(SS))])
        V_NC = e.c(SV)
        e.op("tensor", lambda t: t.matmul(
            psC[:, 0:4], t_["ones_row_f"][:], rows[0:1, R_C4:R_C4 + 4],
            start=True, stop=True), waits=[(SV, V_NC)], inc=("m_nc", 1))
        e.op("vector", lambda v: v.tensor_copy(t_["negc"][:, 0:4], psC[:, 0:4]),
             waits=[("m_nc", e.c("m_nc"))])
        V_NCD = e.c(SV)

        quads = [[0, 2, 4, 6], [8, 10, 12, 14], [16, 18, 20, 22], [24, 26, 28, 30],
                 [1, 3, 5, 7], [9, 11, 13, 15], [17, 19, 21, 23], [25, 27, 29, 31]]
        e.wait("scalar", [(SV, V_NCD)])
        e.wait("tensor", [(SV, V_NCD)])
        for i_t in range(4):
            tl = htiles[i_t]
            for quad in quads:
                nq = e.c("m_s")
                par = (quad[0] % 2) * 64
                sbank = psA[:, (nq % 2) * 512:((nq % 2) + 1) * 512]
                pbank = psA[0:16, 1024 + (nq % 2) * 512: 1024 + ((nq % 2) + 1) * 512]
                e.wait("tensor", [(SS, e.markv("exp", nq - 2))])
                for i, h in enumerate(quad):
                    ch = h // 2
                    kap = xnt[par:par + 64,
                              KOF + ch * 512 + i_t * 128: KOF + ch * 512 + (i_t + 1) * 128]
                    qap = xnt[par:par + 64,
                              QOF + ch * 512 + i_t * 128: QOF + ch * 512 + (i_t + 1) * 128]
                    e.op("tensor", lambda t, i=i, kap=kap, qap=qap, sbank=sbank: t.matmul(
                        sbank[:, i * 128:(i + 1) * 128], kap, qap, start=True, stop=True))
                    e.op("tensor", lambda t, i=i, ch=ch, par=par, qap=qap, pbank=pbank: t.matmul(
                        pbank[:, i * 128:(i + 1) * 128],
                        t_["pmk"][par:par + 64, ch * 16:(ch + 1) * 16], qap,
                        start=True, stop=True),
                        inc=(("m_s", 1) if i == 3 else None))
                eslot = t_["etok"][:, (nq % 2) * 512:((nq % 2) + 1) * 512]
                e.op("scalar", lambda s, eslot=eslot, sbank=sbank, i_t=i_t: s.activation(
                    eslot, sbank, AF.Exp, bias=t_["negc"][:, i_t:i_t + 1]),
                    waits=[("m_s", e.c("m_s")),
                           ("m_ctx", max(0, e.c("m_ctx") - 1))])
                e.op("scalar", lambda s, pbank=pbank, i_t=i_t: s.activation(
                    t_["epm"][0:16, :], pbank, AF.Exp, bias=t_["negc"][0:16, i_t:i_t + 1]),
                    waits=[("m_ctx", e.c("m_ctx"))])
                e.mark("exp", SS)
                e.op("vector", lambda v, eslot=eslot: v.tensor_mul(
                    eslot, eslot, t_["mtok"][:]), waits=[(SS, e.c(SS))])
                e.op("vector", lambda v: v.tensor_mul(
                    t_["epm"][0:16, :], t_["epm"][0:16, :], t_["mpm"][:]))
                V_MSK = e.c(SV)
                cbank = psB[:, (nq % 2) * 512:(nq % 2) * 512 + 260]
                e.wait("tensor", [(SV, V_MSK)])
                for i, h in enumerate(quad):
                    e.op("tensor", lambda t, i=i, h=h, cbank=cbank, eslot=eslot, i_t=i_t: t.matmul(
                        cbank[:, i * 65:i * 65 + 64], eslot[:, i * 128:(i + 1) * 128],
                        scrb[:, V_OF + i_t * 2048 + h * 64: V_OF + i_t * 2048 + (h + 1) * 64],
                        start=True, stop=False))
                    e.op("tensor", lambda t, i=i, h=h, cbank=cbank: t.matmul(
                        cbank[:, i * 65:i * 65 + 64], t_["epm"][:, i * 128:(i + 1) * 128],
                        t_["pmv"][:, h * 64:(h + 1) * 64], start=False, stop=True))
                    e.op("tensor", lambda t, i=i, cbank=cbank, eslot=eslot: t.matmul(
                        cbank[:, i * 65 + 64:i * 65 + 65], eslot[:, i * 128:(i + 1) * 128],
                        t_["ones_col"][:], start=True, stop=False))
                    e.op("tensor", lambda t, i=i, cbank=cbank: t.matmul(
                        cbank[:, i * 65 + 64:i * 65 + 65], t_["epm"][:, i * 128:(i + 1) * 128],
                        t_["ones_col"][:], start=False, stop=True),
                        inc=(("m_ctx", 1) if i == 3 else None))
                for i, h in enumerate(quad):
                    e.op("vector", lambda v, i=i, cbank=cbank: v.reciprocal(
                        stats[:, 8:9], cbank[:, i * 65 + 64:i * 65 + 65]),
                        waits=[("m_ctx", e.c("m_ctx")), ("m_cf", e.c("m_cf"))])
                    e.op("vector", lambda v, i=i, h=h, cbank=cbank: v.tensor_scalar_mul(
                        t_["ctxt"][:, h * 64:(h + 1) * 64], cbank[:, i * 65:i * 65 + 64],
                        stats[:, 8:9]))
            V_CTXT = e.c(SV)
            transpose_128(
                e, t_,
                lambda ch: t_["ctxt"][:, ch * 128:(ch + 1) * 128],
                lambda ch, i_t=i_t: t_["ctxf"][:, i_t * 2048 + ch * 128:
                                               i_t * 2048 + (ch + 1) * 128],
                NCH, "m_cf", extra_tensor_waits=[(SV, V_CTXT)])
        V_CF = e.c(SV)
        e.wait("tensor", [(SV, V_CF)])
        for cg in range(4):
            for k in range(NCH):
                seq = e.c("s_ow") // 16
                slot = wbuf[:, (seq % 2) * 512:((seq % 2) + 1) * 512]
                e.dma("gpsimd", slot,
                      d_["w_out"][k * 128:(k + 1) * 128, cg * 512:(cg + 1) * 512],
                      "s_ow", waits=[("m_ow", max(0, e.c("m_ow") - 1)),
                                     ("m_qk", e.c("m_qk"))], chain=True)
                wv_ = ([(SV, e.c(SV))] if k == 0 else [])
                e.wait("tensor", [("s_ow", (seq + 1) * 16)] + wv_)
                for i_t in range(4):
                    e.op("tensor", lambda t, i_t=i_t, k=k, slot=slot: t.matmul(
                        psA[:, i_t * 512:(i_t + 1) * 512],
                        t_["ctxf"][:, i_t * 2048 + k * 128: i_t * 2048 + (k + 1) * 128],
                        slot, start=(k == 0), stop=(k == NCH - 1)),
                        inc=(("m_ow", 1) if i_t == 3 else None))
            base_mo = e.c("m_ow")
            for i_t in range(4):
                tl = htiles[i_t]
                e.op("vector", lambda v, i_t=i_t, tl=tl, cg=cg: v.tensor_add(
                    x[:, tl * D + cg * 512: tl * D + (cg + 1) * 512],
                    x[:, tl * D + cg * 512: tl * D + (cg + 1) * 512],
                    psA[:, i_t * 512:(i_t + 1) * 512]),
                    waits=[("m_ow", base_mo)])




NH = 16
HD = 128
NE = 8
NI2 = 11


def build_l1_diag_mask(core):
    c = core % 4
    m = np.zeros((128, 4, 128), dtype=np.float32)
    for j in range(4):
        if j < c:
            m[:, j, :] = 1.0
        elif j == c:
            m[:, j, :] = (np.arange(128)[:, None] <= np.arange(128)[None, :])
    return np.ascontiguousarray(m.reshape(128, 512).astype(BF))


def prep_l1_weights(inp):
    w1 = np.asarray(inp["L1_ffn_w1"])
    w2 = np.asarray(inp["L1_ffn_w2"])
    return {
        "wq_p": pack_lhsT_tiles(np.asarray(inp["L1_attn_W_query"])),
        "wk_p": pack_lhsT_tiles(np.asarray(inp["L1_attn_W_key"])),
        "wv": np.ascontiguousarray(np.asarray(inp["L1_attn_W_value"])),
        "w_out": np.ascontiguousarray(np.asarray(inp["L1_attn_W_out"])),
        "b_out": np.ascontiguousarray(np.asarray(inp["L1_attn_b_out"]).reshape(1, D)),
        "router_t": np.ascontiguousarray(np.asarray(inp["L1_ffn_router_w"]).T),
        "w1_p": np.ascontiguousarray(
            np.stack([pack_lhsT_tiles(w1[e].T) for e in range(NE)])),
        "w2_t": np.ascontiguousarray(w2.transpose(0, 2, 1)),
        "ident": np.ascontiguousarray(np.eye(128, dtype=BF)),
    }


L1B_W_SPECS = {
    "wk_p": ([16, 128, 2048], F32), "wv": ([D, D], F32),
    "router_t": ([D, 8], F32), "w1_p": ([8, 22, 128, 2048], F32),
    "w2_t": ([8, 1408, 2048], F32), "ident": ([128, 128], BF16),
}

L1C_W_SPECS = {
    "wq_p": ([16, 128, 2048], F32), "w_out": ([D, D], F32),
    "b_out": ([1, D], F32), "ident": ([128, 128], BF16),
}



def declare_l1b_tensors(nc):
    d_ = {
        "x": nc.dram_tensor("x_in", [TCORE, D], F32, kind="ExternalInput"),
        "x2": nc.dram_tensor("x2", [TCORE, D], F32, kind="ExternalOutput"),
        "xnf_out": nc.dram_tensor("xnf_out", [128, NCH * 1024], BF16, kind="ExternalOutput"),
        "kin": nc.dram_tensor("kin", [TCORE, D], BF16, kind="ExternalOutput"),
        "vin": nc.dram_tensor("vin", [TCORE, D], BF16, kind="ExternalOutput"),
        "nout": nc.dram_tensor("nout", [1, 16], F32, kind="ExternalOutput"),
    }
    for name, (shape, dt) in L1B_W_SPECS.items():
        d_[name] = nc.dram_tensor("w_" + name, shape, dt, kind="ExternalInput")
    A = nc.alloc_sbuf_tensor
    t_ = {"dram": d_}
    t_["x"] = A("x_sb", [128, NTILE * D], F32)
    t_["bfA"] = A("bfA", [128, NCH * 1024], BF16)
    t_["bfB"] = A("bfB", [128, NTILE * D], BF16)
    t_["scr"] = A("scr", [128, 4608], F32)
    t_["scrb"] = t_["scr"].bitcast(BF16)
    t_["wbuf"] = A("wbuf", [128, 9216], BF16)
    t_["gall"] = A("gall", [128, 64], F32)
    t_["ident"] = A("ident_sb", [128, 128], BF16)
    t_["ones_col"] = A("ones_col", [128, 1], BF16)
    t_["stats"] = A("stats", [128, 64], F32)
    t_["rows"] = A("rows", [1, 64], F32)
    t_["knloc"] = A("knloc", [1, 16], F32)
    t_["consts"] = A("consts", [128, 2], F32)
    t_["psA"] = nc.alloc_psum_tensor("psA", [128, 2048], F32)
    t_["psB"] = nc.alloc_psum_tensor("psB", [128, 1024], F32)
    t_["psC"] = nc.alloc_psum_tensor("psC", [128, 512], F32)
    t_["psD"] = nc.alloc_psum_tensor("psD", [128, 512], F32)
    t_["psCb"] = t_["psC"].bitcast(BF16)
    t_["psDb"] = t_["psD"].bitcast(BF16)
    return t_


def _ypan_b(t_, tl):
    if tl < 4:
        return t_["psA"][:, tl * 512:(tl + 1) * 512]
    if tl < 6:
        return t_["psB"][:, (tl - 4) * 512:(tl - 3) * 512]
    return (t_["psC"] if tl == 6 else t_["psD"])[:, 0:512]


def emit_ln(e, t_):
    x, xnt, xnf = t_["x"], t_["bfB"], t_["bfA"]
    scr, stats = t_["scr"], t_["stats"]
    e.wait("scalar", [("s_xld", 16)])
    e.wait("vector", [("s_xld", 16)])
    for tl in range(NTILE):
        xt = x[:, tl * D:(tl + 1) * D]
        e.op("scalar", lambda s, xt=xt: s.activation(
            scr[:, 0:2048], xt, AF.Square, accum_out=stats[:, 0:1]))
        e.op("vector", lambda v, xt=xt: v.reduce_sum(stats[:, 1:2], xt, axis=AX))
        e.op("vector", lambda v: v.tensor_scalar_mul(stats[:, 2:3], stats[:, 1:2], 1.0 / D))
        e.op("vector", lambda v: v.tensor_mul(stats[:, 3:4], stats[:, 2:3], stats[:, 2:3]))
        e.op("vector", lambda v: v.scalar_tensor_tensor(
            stats[:, 3:4], stats[:, 0:1], 1.0 / D, stats[:, 3:4], ALU.mult, ALU.subtract),
            waits=[(SS, e.c(SS))])
        e.op("scalar", lambda s: s.activation(
            stats[:, 3:4], stats[:, 3:4], AF.Ln, bias=t_["consts"][:, 0:1]),
            waits=[(SV, e.c(SV))])
        e.op("scalar", lambda s: s.activation(stats[:, 4:5], stats[:, 3:4], AF.Exp, scale=-0.5))
        e.op("vector", lambda v: v.scalar_tensor_tensor(
            stats[:, 5:6], stats[:, 2:3], -1.0, stats[:, 4:5], ALU.mult, ALU.mult),
            waits=[(SS, e.c(SS))])
        xnt_t = xnt[:, tl * D:(tl + 1) * D]
        e.op("scalar", lambda s, xt=xt, xnt_t=xnt_t: s.activation(
            xnt_t, xt, AF.Identity, bias=stats[:, 5:6], scale=stats[:, 4:5]),
            waits=[(SV, e.c(SV))])
        S_LN = e.c(SS)
        transpose_128(
            e, t_,
            lambda ch, xnt_t=xnt_t: xnt_t[:, ch * 128:(ch + 1) * 128],
            lambda ch, tl=tl: xnf[:, ch * 1024 + tl * 128: ch * 1024 + (tl + 1) * 128],
            NCH, "m_tr",
            extra_tensor_waits=[(SS, S_LN), ("s_cld", e.c("s_cld"))])


def emit_gating(e, t_, scale2=False):
    scr, stats = t_["scr"], t_["stats"]
    lg3 = scr[:, 0:64].rearrange("p (t e) -> p t e", e=8)
    is1 = scr[:, 64:128].rearrange("p (t e) -> p t e", e=8)
    l2 = scr[:, 128:192].rearrange("p (t e) -> p t e", e=8)
    is2 = scr[:, 192:256].rearrange("p (t e) -> p t e", e=8)
    g3 = t_["gall"][:].rearrange("p (t e) -> p t e", e=8)
    m1, m2 = stats[:, 8:16], stats[:, 16:24]
    d12, p1, p2 = stats[:, 24:32], stats[:, 32:40], stats[:, 40:48]
    m1b = m1.rearrange("p (t o) -> p t o", o=1).broadcast_to([128, 8, 8])
    m2b = m2.rearrange("p (t o) -> p t o", o=1).broadcast_to([128, 8, 8])
    p1b = p1.rearrange("p (t o) -> p t o", o=1).broadcast_to([128, 8, 8])
    p2b = p2.rearrange("p (t o) -> p t o", o=1).broadcast_to([128, 8, 8])
    hl3 = t_["psB"][:, 0:8 * 128].rearrange("p (t f) -> p t f", f=128)
    e.op("vector", lambda v: v.tensor_copy(lg3, hl3[:, :, 0:8]), waits=[("m_hl", 1)])
    e.op("vector", lambda v: v.reduce_max(m1, lg3, axis=AX))
    e.op("vector", lambda v: v.tensor_tensor(is1, lg3, m1b, ALU.is_ge))
    e.op("vector", lambda v: v.scalar_tensor_tensor(l2, is1, -1e30, lg3, ALU.mult, ALU.add))
    e.op("vector", lambda v: v.reduce_max(m2, l2, axis=AX))
    e.op("vector", lambda v: v.tensor_sub(d12, m1, m2))
    e.op("scalar", lambda s: s.activation(p1, d12, AF.Sigmoid), waits=[(SV, e.c(SV))])
    e.op("vector", lambda v: v.tensor_scalar(p2, p1, -1.0, 1.0, ALU.mult, ALU.add),
         waits=[(SS, e.c(SS))])
    e.op("vector", lambda v: v.tensor_tensor(is2, l2, m2b, ALU.is_ge))
    e.op("vector", lambda v: v.tensor_tensor(is1, is1, p1b, ALU.mult))
    e.op("vector", lambda v: v.tensor_tensor(is2, is2, p2b, ALU.mult))
    e.op("vector", lambda v: v.tensor_add(g3, is1, is2))
    if scale2:
        e.op("vector", lambda v: v.tensor_scalar_mul(t_["gall"][:], t_["gall"][:], 2.0))


def emit_l1b(e, t_):
    d_ = t_["dram"]
    x, xnf, xnt = t_["x"], t_["bfA"], t_["bfB"]
    scr, scrb, wbuf = t_["scr"], t_["scrb"], t_["wbuf"]
    psA, psB, psC, psD = t_["psA"], t_["psB"], t_["psC"], t_["psD"]
    stats, rows = t_["stats"], t_["rows"]

    e.op("gpsimd", lambda g: g.memset(t_["ones_col"][:], 1.0))
    e.op("gpsimd", lambda g: g.memset(t_["consts"][:, 0:1], 1e-5))
    e.dma("sync", x[:].rearrange("p (t d) -> p t d", d=D),
          d_["x"].rearrange("(t p) d -> p t d", p=128), "s_xld")
    e.dma("sync", t_["ident"][:], d_["ident"][:], "s_cld")
    e.dma("gpsimd", scrb[:, 8704:8832].rearrange("p (k f) -> p k f", f=8),
          d_["router_t"].rearrange("(k p) f -> p k f", p=128), "s_cld", chain=True)
    NCLD = e.c("s_cld")

    emit_ln(e, t_)
    V_XNF = e.c(SV)

    e.wait("tensor", [(SV, V_XNF), ("s_cld", NCLD)])
    for tl in range(NTILE):
        for k in range(NCH):
            e.op("tensor", lambda t, tl=tl, k=k: t.matmul(
                psB[:, tl * 128:tl * 128 + 8],
                xnf[:, k * 1024 + tl * 128: k * 1024 + (tl + 1) * 128],
                scrb[:, 8704 + k * 8: 8704 + (k + 1) * 8],
                start=(k == 0), stop=(k == NCH - 1)),
                inc=(("m_hl", 1) if (tl == NTILE - 1 and k == NCH - 1) else None))
    emit_gating(e, t_, scale2=False)
    V_GATE = e.c(SV)

    for j in range(NH):
        seq = e.c("s_qkw") // 16
        slot = wbuf[:, (seq % 2) * 2048:((seq % 2) + 1) * 2048]
        e.dma("gpsimd", slot, d_["wk_p"][j], "s_qkw",
              waits=[("m_qk", max(0, 2 * (seq - 1)))], chain=True)
        for g in range(2):
            gi = 2 * seq + g
            pss = psB[:, (gi % 2) * 512:((gi % 2) + 1) * 512]
            w0 = [(SV, V_GATE)] if gi == 0 else []
            e.wait("tensor", [("s_qkw", (seq + 1) * 16),
                              (SS, e.markv("kevac", gi - 2))] + w0)
            for k in range(NCH):
                e.op("tensor", lambda t, k=k, g=g, slot=slot, pss=pss: t.matmul(
                    pss, slot[:, k * 128:(k + 1) * 128],
                    xnf[:, k * 1024 + g * 512: k * 1024 + g * 512 + 512],
                    start=(k == 0), stop=(k == NCH - 1)),
                    inc=(("m_qk", 1) if k == NCH - 1 else None))
            ktile = scrb[:, (gi % 4) * 512:((gi % 4) + 1) * 512]
            e.op("scalar", lambda s, pss=pss, ktile=ktile: s.activation(
                ktile, pss, AF.Copy, scale=1.0),
                waits=[("m_qk", e.c("m_qk")), ("s_kw", max(0, e.c("s_kw") - 3 * 64))])
            e.mark("kevac", SS)
            KEV = e.c(SS)
            e.op("scalar", lambda s, ktile=ktile: s.activation(
                scrb[:, 2048:2560], ktile, AF.Square))
            S_SQ = e.c(SS)
            e.op("tensor", lambda t: t.matmul(
                psC[0:1, 0:512], t_["ones_col"][:], scrb[:, 2048:2560],
                start=True, stop=True), waits=[(SS, S_SQ), (SV, e.c(SV))],
                inc=("m_n2", 1))
            e.op("vector", lambda v: v.reduce_max(
                rows[0:1, 16:20], psC[0:1, 0:512].rearrange("o (a b) -> o a b", a=4),
                axis=AX), waits=[("m_n2", e.c("m_n2"))])
            e.op("vector", lambda v: v.reduce_max(
                rows[0:1, 20:21], rows[0:1, 16:20].rearrange("o (a b) -> o a b", a=1),
                axis=AX))
            if g == 0:
                e.op("vector", lambda v, j=j: v.tensor_copy(
                    t_["knloc"][0:1, j:j + 1], rows[0:1, 20:21]))
            else:
                e.op("vector", lambda v, j=j: v.tensor_max(
                    t_["knloc"][0:1, j:j + 1], t_["knloc"][0:1, j:j + 1],
                    rows[0:1, 20:21]))
            for lt in range(4):
                e.dma("sync",
                      d_["kin"][(4 * g + lt) * 128:(4 * g + lt + 1) * 128,
                                j * 128:(j + 1) * 128],
                      ktile[:, lt * 128:(lt + 1) * 128], "s_kw",
                      waits=[(SS, KEV)])
    e.dma("sync", d_["nout"][:], t_["knloc"][:], "s_kw", waits=[(SV, e.c(SV))])

    vin3 = d_["vin"].rearrange("(lt p) d -> p lt d", p=128)
    for gh in range(2):
        for g4 in range(4):
            for k in range(NCH):
                seq = e.c("s_vw") // 16
                slot = wbuf[:, 4096 + (seq % 2) * 512:4096 + ((seq % 2) + 1) * 512]
                e.dma("gpsimd", slot,
                      d_["wv"][k * 128:(k + 1) * 128, g4 * 512:(g4 + 1) * 512],
                      "s_vw", waits=[("m_vk", max(0, e.c("m_vk") - 1))], chain=True)
                wv_ = ([(SV, e.c(SV))] if k == 0 else [])
                e.wait("tensor", [("s_vw", (seq + 1) * 16)] + wv_)
                for i in range(4):
                    e.op("tensor", lambda t, i=i, k=k, gh=gh, slot=slot: t.matmul(
                        psA[:, i * 512:(i + 1) * 512],
                        xnf[:, k * 1024 + gh * 512 + i * 128:
                            k * 1024 + gh * 512 + (i + 1) * 128],
                        slot, start=(k == 0), stop=(k == NCH - 1)),
                        inc=(("m_vk", 1) if i == 3 else None))
            base_mv = e.c("m_vk")
            for i in range(4):
                e.op("vector", lambda v, i=i, g4=g4: v.tensor_copy(
                    xnt[:, i * 2048 + g4 * 512: i * 2048 + (g4 + 1) * 512],
                    psA[:, i * 512:(i + 1) * 512]),
                    waits=[("m_vk", base_mv), ("s_vo", e.c("s_vo"))])
        for i in range(4):
            e.dma("sync", vin3[:, gh * 4 + i, :], xnt[:, i * 2048:(i + 1) * 2048],
                  "s_vo", waits=[(SV, e.c(SV))])

    yact = xnt
    for ee in range(NE):
        for j in range(NI2):
            pr = ee * NI2 + j
            seqg = e.c("s_w1") // 16
            slg = wbuf[:, (seqg % 4) * 2048:((seqg % 4) + 1) * 2048]
            e.dma("gpsimd", slg, d_["w1_p"][ee, j], "s_w1",
                  waits=[("m_y1", max(0, e.c("m_y1") - 1)), ("m_vk", e.c("m_vk")),
                         ("m_qk", e.c("m_qk"))], chain=True)
            sequ = e.c("s_w1") // 16
            slu = wbuf[:, (sequ % 4) * 2048:((sequ % 4) + 1) * 2048]
            e.dma("gpsimd", slu, d_["w1_p"][ee, j + NI2], "s_w1",
                  waits=[("m_y1", max(0, e.c("m_y1") - 1))], chain=True)
            wv_ = ([(SV, e.c(SV))] if pr == 0 else [])
            e.wait("tensor", [("s_w1", (sequ + 1) * 16),
                              (SS, e.markv("ya", pr - 1)),
                              (SV, e.markv("yam", pr - 1))] + wv_)
            for g in range(2):
                for k in range(NCH):
                    e.op("tensor", lambda t, k=k, g=g, slg=slg: t.matmul(
                        psA[:, g * 512:(g + 1) * 512], slg[:, k * 128:(k + 1) * 128],
                        xnf[:, k * 1024 + g * 512: k * 1024 + g * 512 + 512],
                        start=(k == 0), stop=(k == NCH - 1)))
                for k in range(NCH):
                    e.op("tensor", lambda t, k=k, g=g, slu=slu: t.matmul(
                        psA[:, 1024 + g * 512: 1024 + (g + 1) * 512],
                        slu[:, k * 128:(k + 1) * 128],
                        xnf[:, k * 1024 + g * 512: k * 1024 + g * 512 + 512],
                        start=(k == 0), stop=(k == NCH - 1)),
                        inc=(("m_y1", 1) if (k == NCH - 1 and g == 1) else None))
            e.op("scalar", lambda s: s.activation(
                scr[:, 0:1024], psA[:, 0:1024], AF.Silu),
                waits=[("m_y1", e.c("m_y1")), (SV, e.c(SV))])
            e.mark("ya", SS)
            e.op("vector", lambda v, j=j: v.tensor_mul(
                yact[:, j * 1024:(j + 1) * 1024], scr[:, 0:1024], psA[:, 1024:2048]),
                waits=[(SS, e.c(SS))])
            e.mark("yam", SV)
        for cseg in range(4):
            for j in range(NI2):
                seq2 = e.c("s_w2") // 16
                sl2 = wbuf[:, 8192 + (seq2 % 2) * 512:8192 + ((seq2 % 2) + 1) * 512]
                e.dma("gpsimd", sl2,
                      d_["w2_t"][ee, j * 128:(j + 1) * 128, cseg * 512:(cseg + 1) * 512],
                      "s_w2", waits=[("m_y2", max(0, e.c("m_y2") - 1))], chain=True)
                wv_ = ([(SV, e.c(SV))] if j == 0 else [])
                e.wait("tensor", [("s_w2", (seq2 + 1) * 16)] + wv_)
                for tl in range(NTILE):
                    e.op("tensor", lambda t, tl=tl, j=j, sl2=sl2: t.matmul(
                        _ypan_b(t_, tl),
                        yact[:, j * 1024 + tl * 128: j * 1024 + (tl + 1) * 128],
                        sl2, start=(j == 0), stop=(j == NI2 - 1)),
                        inc=(("m_y2", 1) if tl == NTILE - 1 else None))
            for tl in range(NTILE):
                gcol = t_["gall"][:, tl * 8 + ee: tl * 8 + ee + 1]
                e.op("vector", lambda v, tl=tl, cseg=cseg, gcol=gcol: v.scalar_tensor_tensor(
                    x[:, tl * D + cseg * 512: tl * D + (cseg + 1) * 512],
                    _ypan_b(t_, tl), gcol,
                    x[:, tl * D + cseg * 512: tl * D + (cseg + 1) * 512],
                    ALU.mult, ALU.add),
                    waits=[("m_y2", e.c("m_y2"))])

    e.dma("sync", d_["xnf_out"][:], xnf[:], "s_xout", waits=[(SV, e.c(SV))])
    e.dma("sync", d_["x2"].rearrange("(t p) d -> p t d", p=128),
          x[:].rearrange("p (t d) -> p t d", d=D), "s_xout", waits=[(SV, e.c(SV))])
    e.wait("sync", [("s_xout", e.c("s_xout")), ("s_kw", e.c("s_kw")),
                    ("s_vo", e.c("s_vo"))])



def declare_l1c_tensors(nc):
    d_ = {
        "x2": nc.dram_tensor("x2_in", [TCORE, D], F32, kind="ExternalInput"),
        "xnf_in": nc.dram_tensor("xnf_in", [128, NCH * 1024], BF16, kind="ExternalInput"),
        "kg": nc.dram_tensor("kg", [4 * TCORE, D], BF16, kind="ExternalInput"),
        "vg": nc.dram_tensor("vg", [4 * TCORE, D], BF16, kind="ExternalInput"),
        "ng": nc.dram_tensor("ng", [16, 4], F32, kind="ExternalInput"),
        "dmask": nc.dram_tensor("dmask", [128, 512], BF16, kind="ExternalInput"),
        "x3": nc.dram_tensor("x3", [TCORE, D], F32, kind="ExternalOutput"),
    }
    for name, (shape, dt) in L1C_W_SPECS.items():
        d_[name] = nc.dram_tensor("w_" + name, shape, dt, kind="ExternalInput")
    A = nc.alloc_sbuf_tensor
    t_ = {"dram": d_}
    t_["xnf"] = A("xnf", [128, NCH * 1024], BF16)
    t_["qall"] = A("qall", [128, NCH * 1024], BF16)
    t_["ctxf"] = A("ctxf", [128, NTILE * D], BF16)
    t_["kblk"] = A("kblk", [128, 2 * 4 * 2048], BF16)
    t_["vblk"] = A("vblk", [128, 2 * 4 * 2048], BF16)
    t_["acc"] = A("acc", [128, 2 * 16 * 132], F32)
    t_["ctxt"] = A("ctxt", [128, D], BF16)
    t_["eb"] = A("eb", [128, 1024], BF16)
    t_["dmask"] = A("dmask_sb", [128, 512], BF16)
    t_["wbuf"] = A("wbuf", [128, 4096], BF16)
    t_["x2t"] = A("x2t", [128, 1024], F32)
    t_["negcb"] = A("negcb", [128, 128], F32)
    t_["bout"] = A("bout", [1, 2048], BF16)
    t_["ident"] = A("ident_sb", [128, 128], BF16)
    t_["ones_col"] = A("ones_col", [128, 1], BF16)
    t_["ones_row_b"] = A("ones_row_b", [1, 128], BF16)
    t_["ones_row_f"] = A("ones_row_f", [1, 128], F32)
    t_["stats"] = A("stats", [128, 64], F32)
    t_["rows"] = A("rows", [1, 256], F32)
    t_["psS"] = nc.alloc_psum_tensor("psS", [128, 1024], F32)
    t_["psT"] = nc.alloc_psum_tensor("psT", [128, 1024], F32)
    t_["psB"] = nc.alloc_psum_tensor("psB", [128, 1024], F32)
    t_["psC"] = nc.alloc_psum_tensor("psC", [128, 512], F32)
    t_["psD"] = nc.alloc_psum_tensor("psD", [128, 512], F32)
    t_["psCb"] = t_["psC"].bitcast(BF16)
    t_["psDb"] = t_["psD"].bitcast(BF16)
    return t_


def _ypanc(t_, tl):
    if tl < 2:
        return t_["psB"][:, tl * 512:(tl + 1) * 512]
    if tl < 4:
        return t_["psS"][:, (tl - 2) * 512:(tl - 1) * 512]
    if tl < 6:
        return t_["psT"][:, (tl - 4) * 512:(tl - 3) * 512]
    return (t_["psC"] if tl == 6 else t_["psD"])[:, 0:512]


def emit_l1c(e, t_):
    d_ = t_["dram"]
    xnf, qall, ctxf = t_["xnf"], t_["qall"], t_["ctxf"]
    kblk, vblk, acc = t_["kblk"], t_["vblk"], t_["acc"]
    psS, psT, psB = t_["psS"], t_["psT"], t_["psB"]
    psC, psD = t_["psC"], t_["psD"]
    stats, rows, wbuf = t_["stats"], t_["rows"], t_["wbuf"]

    e.op("gpsimd", lambda g: g.memset(t_["ones_col"][:], 1.0))
    e.op("gpsimd", lambda g: g.memset(t_["ones_row_b"][:], 1.0))
    e.op("gpsimd", lambda g: g.memset(t_["ones_row_f"][:], 1.0))
    e.dma("sync", t_["xnf"][:], d_["xnf_in"][:], "s_xld")
    e.dma("sync", t_["dmask"][:], d_["dmask"][:], "s_cld")
    e.dma("sync", t_["ident"][:], d_["ident"][:], "s_cld")
    e.dma("sync", rows[0:1, 128:192].rearrange("o (h r) -> o h r", r=4),
          d_["ng"].rearrange("(o h) r -> o h r", o=1), "s_cld")
    e.dma("gpsimd", t_["bout"][:], d_["b_out"][:], "s_cld", chain=True)
    NCLD = e.c("s_cld")
    e.op("vector", lambda v: v.reduce_max(
        rows[0:1, 64:80], rows[0:1, 128:192].rearrange("o (h r) -> o h r", r=4),
        axis=AX), waits=[("s_cld", NCLD)])
    V_KN = e.c(SV)

    e.wait("tensor", [("s_xld", 16)])
    for j in range(NH):
        seq = e.c("s_qkw") // 16
        slot = wbuf[:, (seq % 2) * 2048:((seq % 2) + 1) * 2048]
        e.dma("gpsimd", slot, d_["wq_p"][j], "s_qkw",
              waits=[("m_qk", max(0, 2 * (seq - 1)))], chain=True)
        for g in range(2):
            gi = 2 * seq + g
            pss = psB[:, (gi % 2) * 512:((gi % 2) + 1) * 512]
            e.wait("tensor", [("s_qkw", (seq + 1) * 16),
                              (SS, e.markv("qevac", gi - 2))])
            for k in range(NCH):
                e.op("tensor", lambda t, k=k, g=g, slot=slot, pss=pss: t.matmul(
                    pss, slot[:, k * 128:(k + 1) * 128],
                    xnf[:, k * 1024 + g * 512: k * 1024 + g * 512 + 512],
                    start=(k == 0), stop=(k == NCH - 1)),
                    inc=(("m_qk", 1) if k == NCH - 1 else None))
            dst = qall[:, j * 1024 + g * 512: j * 1024 + (g + 1) * 512]
            e.op("scalar", lambda s, dst=dst, pss=pss: s.activation(
                dst, pss, AF.Copy, scale=float(1.0 / np.sqrt(HD))),
                waits=[("m_qk", e.c("m_qk"))])
            e.mark("qevac", SS)
            e.op("scalar", lambda s, dst=dst: s.activation(
                t_["ctxt"][:, 0:512], dst, AF.Square),
                waits=[("m_n2", e.c("m_n2"))])
            S_SQ = e.c(SS)
            e.op("tensor", lambda t: t.matmul(
                psC[0:1, 0:512], t_["ones_col"][:], t_["ctxt"][:, 0:512],
                start=True, stop=True), waits=[(SS, S_SQ), (SV, e.c(SV))],
                inc=("m_n2", 1))
            e.op("vector", lambda v, j=j, g=g: v.reduce_max(
                rows[0:1, j * 8 + g * 4: j * 8 + (g + 1) * 4],
                psC[0:1, 0:512].rearrange("o (a b) -> o a b", a=4), axis=AX),
                waits=[("m_n2", e.c("m_n2"))])
    e.op("vector", lambda v: v.tensor_tensor(
        rows[0:1, 0:128].rearrange("o (h q) -> o h q", q=8),
        rows[0:1, 0:128].rearrange("o (h q) -> o h q", q=8),
        rows[0:1, 64:80].rearrange("o (h q) -> o h q", q=1).broadcast_to([1, 16, 8]),
        ALU.mult), waits=[(SV, V_KN)])
    e.op("scalar", lambda s: s.activation(rows[0:1, 0:128], rows[0:1, 0:128], AF.Ln),
         waits=[(SV, e.c(SV))])
    e.op("scalar", lambda s: s.activation(rows[0:1, 0:128], rows[0:1, 0:128], AF.Exp, scale=0.5))
    e.op("vector", lambda v: v.tensor_scalar_mul(rows[0:1, 0:128], rows[0:1, 0:128], -1.0),
         waits=[(SS, e.c(SS))])
    V_NC = e.c(SV)
    e.op("tensor", lambda t: t.matmul(
        psC[:, 0:128], t_["ones_row_f"][:], rows[0:1, 0:128], start=True, stop=True),
        waits=[(SV, V_NC)], inc=("m_nc", 1))
    e.op("vector", lambda v: v.tensor_copy(t_["negcb"][:], psC[:, 0:128]),
         waits=[("m_nc", 1)])
    V_NCB = e.c(SV)

    e.wait("scalar", [(SV, V_NCB)])
    step_mctx = []
    nstep_tot = 0
    for qp in range(4):
        lo, hi = 2 * qp, 2 * qp + 1
        for step in range(hi + 1):
            sl = nstep_tot % 2
            guard = step_mctx[nstep_tot - 2] if nstep_tot >= 2 else 0
            for r in range(4):
                e.dma("sync", kblk[:, (sl * 4 + r) * 2048:(sl * 4 + r + 1) * 2048],
                      d_["kg"][r * 1024 + step * 128: r * 1024 + (step + 1) * 128, :],
                      "s_kv", waits=[("m_ctx", guard)])
                e.dma("sync", vblk[:, (sl * 4 + r) * 2048:(sl * 4 + r + 1) * 2048],
                      d_["vg"][r * 1024 + step * 128: r * 1024 + (step + 1) * 128, :],
                      "s_kv")
            nstep_tot += 1
            e.wait("tensor", [("s_kv", e.c("s_kv"))])
            qts = ([lo] if step <= lo else []) + [hi]
            for h in range(NH):
                for qt in qts:
                    nq = e.c("m_s")
                    sb = psS[:, (nq % 2) * 512:((nq % 2) + 1) * 512]
                    e.wait("tensor", [(SS, e.markv("aexp", nq - 2))])
                    for sub in range(4):
                        e.op("tensor", lambda t, sub=sub, h=h, qt=qt, sb=sb, sl=sl: t.matmul(
                            sb[:, sub * 128:(sub + 1) * 128],
                            kblk[:, (sl * 4 + sub) * 2048 + h * 128:
                                 (sl * 4 + sub) * 2048 + (h + 1) * 128],
                            qall[:, h * 1024 + qt * 128: h * 1024 + (qt + 1) * 128],
                            start=True, stop=True),
                            inc=(("m_s", 1) if sub == 3 else None))
                    eslot = t_["eb"][:, (nq % 2) * 512:((nq % 2) + 1) * 512]
                    e.op("scalar", lambda s, eslot=eslot, sb=sb, h=h, qt=qt: s.activation(
                        eslot, sb, AF.Exp,
                        bias=t_["negcb"][:, h * 8 + qt: h * 8 + qt + 1]),
                        waits=[("m_s", e.c("m_s")), ("m_ctx", max(0, e.c("m_ctx") - 1))])
                    e.mark("aexp", SS)
                    S_EXP = e.c(SS)
                    diag = (step == qt)
                    if diag:
                        e.op("vector", lambda v, eslot=eslot: v.tensor_mul(
                            eslot, eslot, t_["dmask"][:]), waits=[(SS, S_EXP)])
                    V_E = e.c(SV)
                    cb = psT[:, (nq % 2) * 512:(nq % 2) * 512 + 129]
                    tw = [(SS, S_EXP), (SV, e.markv("aacc", nq - 2))]
                    if diag:
                        tw.append((SV, V_E))
                    e.wait("tensor", tw)
                    for sub in range(4):
                        e.op("tensor", lambda t, sub=sub, h=h, cb=cb, eslot=eslot, sl=sl: t.matmul(
                            cb[:, 0:128], eslot[:, sub * 128:(sub + 1) * 128],
                            vblk[:, (sl * 4 + sub) * 2048 + h * 128:
                                 (sl * 4 + sub) * 2048 + (h + 1) * 128],
                            start=(sub == 0), stop=False))
                    for sub in range(4):
                        e.op("tensor", lambda t, sub=sub, cb=cb, eslot=eslot: t.matmul(
                            cb[:, 128:129], eslot[:, sub * 128:(sub + 1) * 128],
                            t_["ones_col"][:], start=False, stop=(sub == 3)),
                            inc=(("m_ctx", 1) if sub == 3 else None))
                    a_ap = acc[:, ((qt % 2) * 16 + h) * 132:
                               ((qt % 2) * 16 + h) * 132 + 129]
                    if step == 0:
                        e.op("vector", lambda v, a_ap=a_ap, cb=cb: v.tensor_copy(
                            a_ap, cb[:, 0:129]), waits=[("m_ctx", e.c("m_ctx"))])
                    else:
                        e.op("vector", lambda v, a_ap=a_ap, cb=cb: v.tensor_add(
                            a_ap, a_ap, cb[:, 0:129]), waits=[("m_ctx", e.c("m_ctx"))])
                    e.mark("aacc", SV)
            step_mctx.append(e.c("m_ctx"))
        for qt in (lo, hi):
            for h in range(NH):
                a_ap = acc[:, ((qt % 2) * 16 + h) * 132:
                           ((qt % 2) * 16 + h) * 132 + 129]
                e.op("vector", lambda v, a_ap=a_ap: v.reciprocal(
                    stats[:, 8:9], a_ap[:, 128:129]),
                    waits=[("m_cf", e.c("m_cf"))])
                e.op("vector", lambda v, a_ap=a_ap, h=h: v.tensor_scalar_mul(
                    t_["ctxt"][:, h * 128:(h + 1) * 128], a_ap[:, 0:128], stats[:, 8:9]))
            V_CT = e.c(SV)
            transpose_128(
                e, t_,
                lambda ch: t_["ctxt"][:, ch * 128:(ch + 1) * 128],
                lambda ch, qt=qt: ctxf[:, qt * 2048 + ch * 128: qt * 2048 + (ch + 1) * 128],
                NCH, "m_cf", extra_tensor_waits=[(SV, V_CT)])

    V_CF = e.c(SV)
    e.wait("tensor", [(SV, V_CF)])
    x2v = d_["x2"].rearrange("(t p) d -> p t d", p=128)
    x3v = d_["x3"].rearrange("(t p) d -> p t d", p=128)
    for cg in range(4):
        for k in range(NCH):
            seq = e.c("s_ow") // 16
            slot = wbuf[:, (seq % 2) * 512:((seq % 2) + 1) * 512]
            e.dma("gpsimd", slot,
                  d_["w_out"][k * 128:(k + 1) * 128, cg * 512:(cg + 1) * 512],
                  "s_ow", waits=[("m_ow", max(0, e.c("m_ow") - 1)),
                                 ("m_qk", e.c("m_qk"))], chain=True)
            wv_ = ([(SV, e.c(SV))] if k == 0 else [])
            e.wait("tensor", [("s_ow", (seq + 1) * 16), ("s_cld", NCLD)] + wv_)
            for tl in range(NTILE):
                e.op("tensor", lambda t, tl=tl, k=k, slot=slot: t.matmul(
                    _ypanc(t_, tl),
                    ctxf[:, tl * 2048 + k * 128: tl * 2048 + (k + 1) * 128],
                    slot, start=(k == 0), stop=False),
                    inc=(("m_ow", 1) if tl == NTILE - 1 else None))
        for tl in range(NTILE):
            e.op("tensor", lambda t, tl=tl, cg=cg: t.matmul(
                _ypanc(t_, tl), t_["ones_row_b"][:],
                t_["bout"][0:1, cg * 512:(cg + 1) * 512], start=False, stop=True),
                inc=("m_ob", 1))
        base_ob = e.c("m_ob") - NTILE
        for tl in range(NTILE):
            idx = e.c("s_x2") // 16
            xsl = t_["x2t"][:, (idx % 2) * 512:(idx % 2) * 512 + 512]
            e.dma("sync", xsl, x2v[:, tl, cg * 512:(cg + 1) * 512], "s_x2",
                  waits=[("s_x3", max(0, 16 * (idx - 1)))])
            e.op("vector", lambda v, tl=tl, xsl=xsl: v.tensor_add(
                xsl, xsl, _ypanc(t_, tl)),
                waits=[("m_ob", base_ob + tl + 1), ("s_x2", e.c("s_x2"))])
            e.dma("sync", x3v[:, tl, cg * 512:(cg + 1) * 512], xsl, "s_x3",
                  waits=[(SV, e.c(SV))])
    e.wait("sync", [("s_x3", e.c("s_x3"))])



def _build(declare, emitfn):
    nc = bass.Bass("TRN2", target_bir_lowering=False)
    t_ = declare(nc)
    e = Emit(nc)
    with nc.Block() as blk:
        e.run(blk, lambda e: emitfn(e, t_))
    return nc


def kernel(**inputs):
    inputs = {k: np.asarray(v) for k, v in inputs.items()}
    w0 = prep_l0_weights(inputs)
    w1 = prep_l1_weights(inputs)
    xs = shard_x(inputs["L0_attn_seq"])

    nc0 = _build(declare_l0_tensors, emit_l0_block)
    maps0 = []
    for c in range(8):
        m = {"x_in": xs[c]}
        for name in L0_W_SPECS:
            m["w_" + name] = w0[name]
        maps0.append(m)
    res0 = run_bass_kernel_spmd(nc0, maps0, core_ids=list(range(8)))
    x1 = [res0.results[c]["x1"] for c in range(8)]

    ncb = _build(declare_l1b_tensors, emit_l1b)
    mapsb = []
    for c in range(8):
        m = {"x_in": x1[c]}
        for name in L1B_W_SPECS:
            m["w_" + name] = w1[name]
        mapsb.append(m)
    resb = run_bass_kernel_spmd(ncb, mapsb, core_ids=list(range(8)))
    x2 = [resb.results[c]["x2"] for c in range(8)]
    xnf = [resb.results[c]["xnf_out"] for c in range(8)]
    kin = [resb.results[c]["kin"] for c in range(8)]
    vin = [resb.results[c]["vin"] for c in range(8)]
    nout = [resb.results[c]["nout"] for c in range(8)]

    ncc = _build(declare_l1c_tensors, emit_l1c)
    mapsc = []
    for c in range(8):
        grp = [0, 1, 2, 3] if c < 4 else [4, 5, 6, 7]
        m = {"x2_in": x2[c], "xnf_in": xnf[c],
             "kg": np.ascontiguousarray(np.concatenate([kin[r] for r in grp], axis=0)),
             "vg": np.ascontiguousarray(np.concatenate([vin[r] for r in grp], axis=0)),
             "ng": np.ascontiguousarray(np.stack([nout[r][0] for r in grp], axis=1)),
             "dmask": build_l1_diag_mask(c)}
        for name in L1C_W_SPECS:
            m["w_" + name] = w1[name]
        mapsc.append(m)
    resc = run_bass_kernel_spmd(ncc, mapsc, core_ids=list(range(8)))
    x3 = [resc.results[c]["x3"] for c in range(8)]
    return unshard_x(x3)
